# version 02
# baseline (speedup 1.0000x reference)
import numpy as np
import concourse.bass as bass
import concourse.mybir as mybir
from concourse.bass_utils import run_bass_kernel_spmd
from contextlib import ExitStack

F32 = mybir.dt.float32
BF16 = mybir.dt.bfloat16
AF = mybir.ActivationFunctionType
ALU = mybir.AluOpType
AX = mybir.AxisListType

NCORES = 8
B, T, D = 4, 4096, 1024
TPC = 2048
KC = 8
NTT = 4
TTW = 512
NEG = -1.0e30
SQRTD = float(np.sqrt(np.float32(1024.0)))
EPS_CD = 0.1
CD_ITERS = 50
TH_ITERS = 36
IX_ITERS = 14

ENGINES = ("tensor", "vector", "scalar", "gpsimd", "sync")


class Sem:
    def __init__(self, nc, name):
        self.h = nc.alloc_semaphore(name=name)
        self.n = 0
        self.name = name


class Env:
    def __init__(self, nc, es):
        self.nc = nc
        self.sems = {}
        self.es = es

    def sem(self, name):
        if name not in self.sems:
            self.sems[name] = Sem(self.nc, name)
        return self.sems[name]

    def sbuf(self, name, shape, dt):
        return self.es.enter_context(self.nc.sbuf_tensor(name, shape, dt))


class PB:

    def __init__(self, env, tag):
        self.env = env
        self.tag = tag
        self.ops = {e: [] for e in ENGINES}
        self.tsem = {e: env.sem("tok_" + e) for e in ENGINES}
        self.dsem = {e: env.sem("dma_" + e) for e in ENGINES}

    SERIAL = ("vector", "scalar")

    def emit(self, eng, fn, deps=(), token=False, dma=False, dsem=None, late=False):
        t = None
        k = 16 if dma else 1
        auto = (not dma) and (eng in self.SERIAL or eng == "gpsimd")
        selfwait = None
        if dma:
            s = self.env.sem("dma_" + (dsem or eng))
            s.n += k
            t = (s, None if late else s.n, eng)
        elif token or auto:
            s = self.tsem[eng]
            if auto:
                selfwait = (s, s.n)
            s.n += k
            t = (s, s.n, eng)
        deps = tuple(d for d in deps if d is not None)
        self.ops[eng].append((fn, deps, t, k, selfwait))
        return t

    def build(self):
        nc = self.env.nc
        with nc.Block(self.tag) as blk:
            for eng in ENGINES:
                if not self.ops[eng]:
                    continue

                def body(E, _ops=self.ops[eng], _eng=eng):
                    waited = {}
                    for fn, deps, tok, k, selfwait in _ops:
                        if selfwait is not None and selfwait[1] > waited.get(selfwait[0].name, -1):
                            E.wait_ge(selfwait[0].h, selfwait[1])
                            waited[selfwait[0].name] = selfwait[1]
                        for (s, v, peng) in deps:
                            if v is None:
                                v = s.n
                            if peng == _eng and s.name.startswith("tok_"):
                                continue
                            if waited.get(s.name, -1) >= v:
                                continue
                            E.wait_ge(s.h, v)
                            waited[s.name] = v
                        instr = fn(E)
                        if tok is not None:
                            instr.then_inc(tok[0].h, k)
                getattr(blk, eng)(body)


def _host_consts(core):
    c = {}
    c["ones128x1_f32"] = np.ones((128, 1), np.float32)
    c["ones128x1_bf"] = np.ones((128, 1), np.float32)
    c["ones1x128_f32"] = np.ones((1, 128), np.float32)
    c["ones_row"] = np.ones((1, TTW), np.float32)
    c["ident_f32"] = np.eye(128, dtype=np.float32)
    c["ident_bf"] = np.eye(128, dtype=np.float32)
    oh = np.zeros((8, 8 * 128), np.float32)
    for e in range(8):
        oh[e, e * 128:(e + 1) * 128] = 1.0
    c["onehot8"] = oh
    m_up = np.zeros((128, 128), np.float32)
    m_up[np.triu_indices(128, 1)] = NEG
    c["mask_up"] = m_up
    m_lo = np.zeros((128, 256), np.float32)
    m_lo[:, 128:][np.triu_indices(128, 1)] = NEG
    c["mask_lo"] = m_lo
    hd = 64
    inv_freq = (1.0 / 10000.0 ** (np.arange(0, hd, 2, dtype=np.float32) / np.float32(hd))).astype(np.float32)
    pos = (np.arange(TPC, dtype=np.float32) + np.float32((core % 2) * TPC))
    tt = (pos[None, :].astype(np.float32) * inv_freq[:, None]).astype(np.float32)
    cos = np.cos(tt).astype(np.float32)
    sin = np.sin(tt).astype(np.float32)
    cos64 = np.repeat(cos, 2, axis=0)
    sin64s = np.repeat(sin, 2, axis=0)
    sin64s[0::2, :] *= np.float32(-1.0)
    c["rope_cos"] = np.ascontiguousarray(np.concatenate([cos64, cos64], axis=0))
    c["rope_sin"] = np.ascontiguousarray(np.concatenate([sin64s, sin64s], axis=0))
    p64 = np.zeros((64, 64), np.float32)
    for i in range(0, 64, 2):
        p64[i + 1, i] = 1.0
        p64[i, i + 1] = 1.0
    perm = np.zeros((128, 128), np.float32)
    perm[0:64, 0:64] = p64
    perm[64:128, 64:128] = p64
    c["perm_qk"] = perm
    c["idxg"] = (np.arange(TPC, dtype=np.float32) + np.float32((core % 2) * TPC)).reshape(1, TPC)
    c["idx_row"] = np.arange(4096, dtype=np.float32).reshape(128, 32)
    return c


def _prep_weights(inputs):
    w = {}
    f32 = lambda a: np.ascontiguousarray(np.asarray(a, np.float32))
    w["l0_w1s"] = f32(np.concatenate(
        [np.asarray(inputs["L0_ffn_shared_w1"]).T[None],
         np.asarray(inputs["L0_ffn_spec_w1"]).transpose(0, 2, 1)], 0))
    w["l0_w2s"] = f32(np.concatenate(
        [np.asarray(inputs["L0_ffn_shared_w2"]).T[None],
         np.asarray(inputs["L0_ffn_spec_w2"]).transpose(0, 2, 1)], 0))
    b1 = np.concatenate([np.asarray(inputs["L0_ffn_shared_b1"])[None],
                         np.asarray(inputs["L0_ffn_spec_b1"])], 0)
    w["l0_b1s"] = f32(b1.reshape(9, 32, 128).transpose(2, 0, 1).reshape(128, 9 * 32))
    w["l0_b2sp"] = f32(inputs["L0_ffn_spec_b2"])
    w["l0_b2sh"] = f32(np.asarray(inputs["L0_ffn_shared_b2"]).reshape(1, 1024))
    w["l0_rwT"] = f32(np.asarray(inputs["L0_ffn_router_w"]).T.reshape(8, 128, 8)
                      .transpose(1, 0, 2).reshape(128, 64))
    w["l0_rcs"] = f32(np.asarray(inputs["L0_ffn_router_w"]).sum(-1).reshape(8, 1))
    w["l0_rb"] = f32(np.asarray(inputs["L0_ffn_router_b"]).reshape(8, 1))
    for nm in ("w_q", "w_k", "w_v"):
        w["l0_" + nm] = f32(inputs["L0_attn_" + nm])
    w["l1_wqkvT"] = f32(np.asarray(inputs["L1_attn_Wqkv"]).T)
    w["l1_woutT"] = f32(np.asarray(inputs["L1_attn_Wout"]).T)
    w["l1_w1l"] = f32(inputs["L1_ffn_w1_light"])
    w["l1_w2l"] = f32(inputs["L1_ffn_w2_light"])
    w["l1_w1h"] = f32(inputs["L1_ffn_w1_heavy"])
    w["l1_w2h"] = f32(inputs["L1_ffn_w2_heavy"])
    w["l1_b1l"] = f32(np.asarray(inputs["L1_ffn_b1_light"]).reshape(4, 128).T)
    w["l1_b1h"] = f32(np.asarray(inputs["L1_ffn_b1_heavy"]).reshape(32, 128).T)
    w["l1_b2l"] = f32(np.asarray(inputs["L1_ffn_b2_light"]).reshape(1, 1024))
    w["l1_b2h"] = f32(np.asarray(inputs["L1_ffn_b2_heavy"]).reshape(1, 1024))
    w["l1_gl"] = f32(np.asarray(inputs["L1_ffn_gamma_light"]).reshape(8, 128).T)
    w["l1_gh"] = f32(np.asarray(inputs["L1_ffn_gamma_heavy"]).reshape(8, 128).T)
    w["l1_rt"] = f32(np.asarray(inputs["L1_ffn_routing_token"]).reshape(8, 128).T)
    return w


CONST_SPECS = {
    "ones128x1_f32": ([128, 1], F32), "ones128x1_bf": ([128, 1], BF16),
    "ones1x128_f32": ([1, 128], F32), "ones_row": ([1, TTW], F32),
    "ident_f32": ([128, 128], F32), "ident_bf": ([128, 128], BF16),
    "onehot8": ([8, 8 * 128], F32),
    "mask_up": ([128, 128], F32), "mask_lo": ([128, 256], F32),
    "rope_cos": ([128, TPC], F32), "rope_sin": ([128, TPC], F32),
    "perm_qk": ([128, 128], BF16),
    "idxg": ([1, TPC], F32), "idx_row": ([128, 32], F32),
}

WEIGHT_SPECS = {
    "l0_w1s": [9, 1024, 4096], "l0_w2s": [9, 4096, 1024],
    "l0_b1s": [128, 288], "l0_b2sp": [8, 1024], "l0_b2sh": [1, 1024],
    "l0_rwT": [128, 64], "l0_rcs": [8, 1], "l0_rb": [8, 1],
    "l0_w_q": [1024, 1024], "l0_w_k": [1024, 1024], "l0_w_v": [1024, 1024],
    "l1_wqkvT": [1024, 3072], "l1_woutT": [1024, 1024],
    "l1_w1l": [1024, 512], "l1_w2l": [512, 1024],
    "l1_w1h": [1024, 4096], "l1_w2h": [4096, 1024],
    "l1_b1l": [128, 4], "l1_b1h": [128, 32],
    "l1_b2l": [1, 1024], "l1_b2h": [1, 1024],
    "l1_gl": [128, 8], "l1_gh": [128, 8], "l1_rt": [128, 8],
}

CONST_BASE = ["ones128x1_f32", "ones128x1_bf", "ones1x128_f32", "ones_row",
              "ident_f32", "onehot8"]


def declare_io(nc, wnames, cnames, collectives=()):
    dram = {}
    for nm in wnames:
        dram[nm] = nc.dram_tensor(nm, WEIGHT_SPECS[nm], F32, kind="ExternalInput").ap()
    for nm in cnames:
        shape, _dt = CONST_SPECS[nm]
        dram[nm] = nc.dram_tensor(nm, shape, F32, kind="ExternalInput").ap()
    dram["x_in"] = nc.dram_tensor("x_in", [D, TPC], F32, kind="ExternalInput").ap()
    dram["x_out"] = nc.dram_tensor("x_out", [D, TPC], F32, kind="ExternalOutput").ap()
    for nm, shape in collectives:
        dram[nm] = nc.dram_tensor(nm, shape, F32)
    return dram


LN_SCALE = float(2.0 ** -40)
LN_BIAS = float(1e-5 * 2.0 ** -40)
EXP_BIAS = float(-20.0 * np.log(2.0))
SCALAR_CONSTS = [1e-5, -0.5, 10.0, LN_SCALE, LN_BIAS, EXP_BIAS]


def reg_scalar_consts(env, pb):
    for i, v in enumerate(SCALAR_CONSTS):
        t = env.sbuf("cst_%d" % i, [128, 1], F32)
        pb.emit("gpsimd", lambda E, t=t, v=v: E.memset(t[:, :], v), token=True)
        env.nc.const_aps.register(F32, v, t[:, 0:1])


def load_consts(env, pb, st, dram, cnames):
    for nm in cnames:
        shape, dt = CONST_SPECS[nm]
        key = "c_" + nm
        if key not in st:
            st[key] = env.sbuf(key, shape, dt)
        eng = "gpsimd" if dt == BF16 else "sync"
        st["tok_" + nm] = pb.emit(eng, lambda E, a=st[key], b=dram[nm]: E.dma_start(
            out=a[:, :], in_=b[:, :]), dma=True, dsem="c_" + nm)


def emit_stream_in(env, pb, st, dram):
    src = dram["x_in"].rearrange("(kc p) t -> p kc t", p=128)
    dst = st["stream"][:, :].rearrange("p (kc t) -> p kc t", kc=KC)
    st["tok_stream_in"] = pb.emit("sync", lambda E, src=src, dst=dst: E.dma_start(
        out=dst, in_=src), dma=True, dsem="xin")


def emit_stream_out(env, pb, st, dram):
    dst = dram["x_out"].rearrange("(kc p) t -> p kc t", p=128)
    src = st["stream"][:, :].rearrange("p (kc t) -> p kc t", kc=KC)
    t = pb.emit("sync", lambda E, dst=dst, src=src: E.dma_start(
        out=dst, in_=src), dma=True, dsem="xout")
    pb.emit("sync", lambda E, t=t: E.wait_ge(t[0].h, t[1]))


def run_ln(env, st, rms=False, tag="ln"):
    nc = env.nc
    stream, xln = st["stream"], st["xln"]
    m_sb, rstd_sb = st["m_sb"], st["rstd_sb"]
    ones_bf = st["c_ones128x1_bf"]
    ones1r = st["c_ones1x128_f32"]
    with ExitStack() as es:
        if rms:
            st["rteff_sb"] = es.enter_context(
                nc.sbuf_tensor(tag + "_rteff", [1, TPC], F32))
        xb = es.enter_context(nc.sbuf_tensor(tag + "_xb", [128, 2 * TPC], BF16))
        xsq = es.enter_context(nc.sbuf_tensor(tag + "_xsq", [128, 2 * TPC], BF16))
        v1 = es.enter_context(nc.sbuf_tensor(tag + "_v1", [1, TPC], F32))
        v2 = es.enter_context(nc.sbuf_tensor(tag + "_v2", [1, TPC], F32))

        with ExitStack() as es1:
            ps_s1 = es1.enter_context(nc.psum_tensor([1, TPC], F32))
            ps_s2 = es1.enter_context(nc.psum_tensor([1, TPC], F32))
            pb = PB(env, tag + "a")
            tok_stat = [None] * KC
            for kc in range(KC):
                sl_in = slice(kc * TPC, (kc + 1) * TPC)
                sl_rg = slice((kc % 2) * TPC, (kc % 2 + 1) * TPC)
                dep_ring = [tok_stat[kc - 2]] if kc >= 2 else [st.get("tok_stream_in")]
                t_sq = pb.emit("scalar", lambda E, a=sl_rg, b=sl_in: E.activation(
                    out=xsq[:, a], in_=stream[:, b], func=AF.Square),
                    deps=dep_ring, token=True)
                t_cast = pb.emit("vector", lambda E, a=sl_rg, b=sl_in: E.tensor_copy(
                    out=xb[:, a], in_=stream[:, b]), deps=dep_ring, token=True)
                tok = None
                for sub in range(NTT):
                    sl_t = slice(sub * TTW, (sub + 1) * TTW)
                    sl_r = slice((kc % 2) * TPC + sub * TTW, (kc % 2) * TPC + (sub + 1) * TTW)
                    pb.emit("tensor", lambda E, a=sl_t, b=sl_r, s=(kc == 0), p=(kc == KC - 1): E.matmul(
                        ps_s1[:, a], ones_bf[:, :], xb[:, b], start=s, stop=p),
                        deps=[t_cast, st.get("tok_ones128x1_bf")])
                    tok = pb.emit("tensor", lambda E, a=sl_t, b=sl_r, s=(kc == 0), p=(kc == KC - 1): E.matmul(
                        ps_s2[:, a], ones_bf[:, :], xsq[:, b], start=s, stop=p),
                        deps=[t_sq], token=(sub == NTT - 1))
                tok_stat[kc] = tok
            pb.emit("vector", lambda E: E.tensor_scalar(
                out=m_sb[:, :], in0=ps_s1[:, :], scalar1=1.0 / 1024.0, scalar2=None,
                op0=ALU.mult), deps=[tok_stat[KC - 1]])
            pb.emit("vector", lambda E: E.tensor_scalar(
                out=v1[:, :], in0=ps_s2[:, :], scalar1=1.0 / 1024.0, scalar2=None,
                op0=ALU.mult))
            pb.build()

        with ExitStack() as es2:
            ps_bc = es2.enter_context(nc.psum_tensor([128, 2 * TTW], F32))
            bc = es2.enter_context(nc.sbuf_tensor(tag + "_bc", [128, 2 * TTW], F32))
            xt = es2.enter_context(nc.sbuf_tensor(tag + "_xt", [128, TTW], F32))
            pb = PB(env, tag + "b")
            pb.emit("vector", lambda E: E.tensor_tensor(
                out=v2[:, :], in0=m_sb[:, :], in1=m_sb[:, :], op=ALU.mult))
            t_var = pb.emit("vector", lambda E: E.tensor_tensor(
                out=v1[:, :], in0=v1[:, :], in1=v2[:, :], op=ALU.subtract), token=True)
            pb.emit("scalar", lambda E: E.activation(
                out=v2[:, :], in_=v1[:, :], func=AF.Ln, scale=LN_SCALE, bias=LN_BIAS),
                deps=[t_var])
            t_rstd = pb.emit("scalar", lambda E: E.activation(
                out=rstd_sb[:, :], in_=v2[:, :], func=AF.Exp, scale=-0.5,
                bias=EXP_BIAS), token=True)
            rn = rstd_sb
            if rms:
                pb.emit("vector", lambda E: E.tensor_tensor(
                    out=v2[:, :], in0=rstd_sb[:, :], in1=rstd_sb[:, :], op=ALU.mult),
                    deps=[t_rstd])
                pb.emit("vector", lambda E: E.tensor_scalar(
                    out=v2[:, :], in0=v2[:, :], scalar1=1024.0, scalar2=None,
                    op0=ALU.mult))
                pb.emit("vector", lambda E: E.tensor_tensor(
                    out=v2[:, :], in0=v2[:, :], in1=v1[:, :], op=ALU.mult))
                t3 = pb.emit("vector", lambda E: E.tensor_scalar(
                    out=v2[:, :], in0=v2[:, :], scalar1=1e-24, scalar2=None,
                    op0=ALU.max), token=True)
                pb.emit("scalar", lambda E: E.activation(
                    out=v2[:, :], in_=v2[:, :], func=AF.Ln), deps=[t3])
                t5 = pb.emit("scalar", lambda E: E.activation(
                    out=v2[:, :], in_=v2[:, :], func=AF.Exp, scale=-0.5), token=True)
                pb.emit("vector", lambda E: E.tensor_scalar(
                    out=v2[:, :], in0=v2[:, :], scalar1=SQRTD, scalar2=None,
                    op0=ALU.mult), deps=[t5])
                t_rstd = pb.emit("vector", lambda E: E.tensor_tensor(
                    out=st["rteff_sb"][:, :], in0=rstd_sb[:, :], in1=v2[:, :],
                    op=ALU.mult), token=True)
                rn = st["rteff_sb"]
            t_mrt = pb.emit("vector", lambda E, rn=rn: E.tensor_tensor(
                out=v1[:, :], in0=m_sb[:, :], in1=rn[:, :], op=ALU.mult),
                deps=[t_rstd], token=True)
            prev_bc = None
            tok_xln = [None] * NTT
            for tt in range(NTT):
                sl_t = slice(tt * TTW, (tt + 1) * TTW)
                pb.emit("tensor", lambda E, a=sl_t, rn=rn: E.matmul(
                    ps_bc[:, 0:TTW], ones1r[:, :], rn[:, a], start=True, stop=True),
                    deps=[t_rstd, prev_bc, st.get("tok_ones1x128_f32")])
                t_b2 = pb.emit("tensor", lambda E, a=sl_t: E.matmul(
                    ps_bc[:, TTW:2 * TTW], ones1r[:, :], v1[:, a], start=True, stop=True),
                    deps=[t_mrt], token=True)
                t_bc = pb.emit("vector", lambda E: E.tensor_copy(
                    out=bc[:, :], in_=ps_bc[:, :]), deps=[t_b2], token=True)
                prev_bc = t_bc
                tk = None
                for kc in range(KC):
                    sl_x = slice(kc * TPC + tt * TTW, kc * TPC + (tt + 1) * TTW)
                    pb.emit("vector", lambda E, a=sl_x: E.tensor_tensor(
                        out=xt[:, :], in0=stream[:, a], in1=bc[:, 0:TTW], op=ALU.mult))
                    tk = pb.emit("vector", lambda E, a=sl_x: E.tensor_tensor(
                        out=xln[:, a], in0=xt[:, :], in1=bc[:, TTW:2 * TTW],
                        op=ALU.subtract), token=(kc == KC - 1))
                tok_xln[tt] = tk
            st["tok_xln"] = tok_xln
            pb.build()


def run_router(env, st, dram):
    nc = env.nc
    stream = st["stream"]
    m_sb, rstd_sb = st["m_sb"], st["rstd_sb"]
    comb8 = st["comb8"]
    ones1r = st["c_ones1x128_f32"]
    ident = st["c_ident_f32"]
    with ExitStack() as es:
        rwT = es.enter_context(nc.sbuf_tensor("rt_rwT", [128, 64], F32))
        rcs = es.enter_context(nc.sbuf_tensor("rt_rcs", [8, 1], F32))
        rb = es.enter_context(nc.sbuf_tensor("rt_rb", [8, 1], F32))
        b2sp = es.enter_context(nc.sbuf_tensor("rt_b2sp", [8, 1024], F32))
        b2sh = es.enter_context(nc.sbuf_tensor("rt_b2sh", [1, 1024], F32))
        l8 = es.enter_context(nc.sbuf_tensor("rt_l8", [8, TPC], F32))
        ltm = es.enter_context(nc.sbuf_tensor("rt_ltm", [128, 8], F32))
        w8 = es.enter_context(nc.sbuf_tensor("rt_w8", [128, 8], F32))
        c8 = es.enter_context(nc.sbuf_tensor("rt_c8", [128, 8], F32))
        r1 = es.enter_context(nc.sbuf_tensor("rt_r1", [128, 1], F32))
        r2 = es.enter_context(nc.sbuf_tensor("rt_r2", [128, 1], F32))
        r3 = es.enter_context(nc.sbuf_tensor("rt_r3", [128, 1], F32))
        ps_lg = es.enter_context(nc.psum_tensor([8, TTW], F32))
        ps_m8 = es.enter_context(nc.psum_tensor([8, TTW], F32))
        ps_t1 = es.enter_context(nc.psum_tensor([128, 8], F32))
        ps_t2 = es.enter_context(nc.psum_tensor([8, 128], F32))
        ps_b2 = es.enter_context(nc.psum_tensor([128, TTW], F32))
        ps_bsh = es.enter_context(nc.psum_tensor([128, TTW], F32))
        pb = PB(env, "router")

        t_w = pb.emit("sync", lambda E: E.dma_start(out=rwT[:, :], in_=dram["l0_rwT"][:, :]),
                      dma=True, dsem="rt_rwT")
        t_c = pb.emit("sync", lambda E: E.dma_start(out=rcs[:, :], in_=dram["l0_rcs"][:, :]),
                      dma=True, dsem="rt_rcs")
        t_b = pb.emit("sync", lambda E: E.dma_start(out=rb[:, :], in_=dram["l0_rb"][:, :]),
                      dma=True, dsem="rt_rb")
        t_b2sp = pb.emit("sync", lambda E: E.dma_start(out=b2sp[:, :], in_=dram["l0_b2sp"][:, :]),
                         dma=True, dsem="rt_b2sp")
        t_b2sh = pb.emit("sync", lambda E: E.dma_start(out=b2sh[:, :], in_=dram["l0_b2sh"][:, :]),
                         dma=True, dsem="rt_b2sh")

        lg_tok = [None] * NTT
        for tt in range(NTT):
            sl_t = slice(tt * TTW, (tt + 1) * TTW)
            t_mm = None
            for kc in range(KC):
                t_mm = pb.emit("tensor", lambda E, kc=kc, tt=tt, s=(kc == 0), p=(kc == KC - 1): E.matmul(
                    ps_lg[:, :], rwT[:, kc * 8:(kc + 1) * 8],
                    stream[:, kc * TPC + tt * TTW: kc * TPC + (tt + 1) * TTW],
                    start=s, stop=p), deps=([t_w, lg_tok[tt - 1] if tt else None]
                                            if kc == 0 else ()),
                    token=(kc == KC - 1))
            t_m8 = pb.emit("tensor", lambda E, a=sl_t: E.matmul(
                ps_m8[:, :], ones1r[:, 0:8], m_sb[:, a], start=True, stop=True),
                deps=[lg_tok[tt - 1] if tt else None,
                      st.get("tok_ones1x128_f32")], token=True)
            pb.emit("vector", lambda E, a=sl_t: E.tensor_scalar(
                out=l8[:, a], in0=ps_m8[:, :], scalar1=rcs[:, 0:1], scalar2=None,
                op0=ALU.mult), deps=[t_m8, t_c])
            t_s = pb.emit("vector", lambda E, a=sl_t: E.tensor_tensor(
                out=l8[:, a], in0=ps_lg[:, :], in1=l8[:, a], op=ALU.subtract),
                deps=[t_mm], token=True)
            t_r8 = pb.emit("tensor", lambda E, a=sl_t: E.matmul(
                ps_m8[:, :], ones1r[:, 0:8], rstd_sb[:, a], start=True, stop=True),
                deps=[t_s], token=True)
            pb.emit("vector", lambda E, a=sl_t: E.tensor_tensor(
                out=l8[:, a], in0=l8[:, a], in1=ps_m8[:, :], op=ALU.mult),
                deps=[t_r8])
            lg_tok[tt] = pb.emit("vector", lambda E, a=sl_t: E.tensor_scalar(
                out=l8[:, a], in0=l8[:, a], scalar1=rb[:, 0:1], scalar2=None,
                op0=ALU.add), deps=[t_b], token=True)

        prev_use = None
        comb_tok = None
        for ti in range(TPC // 128):
            sl_c = slice(ti * 128, (ti + 1) * 128)
            t_tr = pb.emit("tensor", lambda E, a=sl_c: E.transpose(
                ps_t1[:, :], l8[:, a], ident[0:8, 0:8]),
                deps=[lg_tok[(ti * 128) // TTW], prev_use, st.get("tok_ident_f32")],
                token=True)
            t_ev = pb.emit("vector", lambda E: E.tensor_copy(out=ltm[:, :], in_=ps_t1[:, :]),
                           deps=[t_tr], token=True)
            t_neg = pb.emit("vector", lambda E: E.reduce_max(r1[:, :], ltm[:, :], axis=AX.X,
                                                             negate=True))
            t_p = pb.emit("scalar", lambda E: E.activation(
                out=ltm[:, :], in_=ltm[:, :], func=AF.Exp, bias=r1[:, 0:1]),
                deps=[t_ev, t_neg], token=True)
            pb.emit("vector", lambda E: E.reduce_max(r1[:, :], ltm[:, :], axis=AX.X),
                    deps=[t_p])
            pb.emit("vector", lambda E: E.tensor_scalar(
                out=w8[:, :], in0=ltm[:, :], scalar1=r1[:, 0:1], scalar2=None,
                op0=ALU.is_ge))
            pb.emit("vector", lambda E: E.scalar_tensor_tensor(
                out=w8[:, :], in0=w8[:, :], scalar=NEG, in1=ltm[:, :],
                op0=ALU.mult, op1=ALU.add))
            pb.emit("vector", lambda E: E.reduce_max(r2[:, :], w8[:, :], axis=AX.X))
            pb.emit("vector", lambda E: E.tensor_tensor(
                out=r3[:, :], in0=r1[:, :], in1=r2[:, :], op=ALU.add))
            pb.emit("vector", lambda E: E.reciprocal(r3[:, :], r3[:, :]))
            pb.emit("vector", lambda E: E.tensor_scalar(
                out=c8[:, :], in0=ltm[:, :], scalar1=r2[:, 0:1], scalar2=None,
                op0=ALU.is_ge))
            pb.emit("vector", lambda E: E.tensor_tensor(
                out=c8[:, :], in0=c8[:, :], in1=ltm[:, :], op=ALU.mult))
            t_c8 = pb.emit("vector", lambda E: E.tensor_scalar(
                out=c8[:, :], in0=c8[:, :], scalar1=r3[:, 0:1], scalar2=0.5,
                op0=ALU.mult, op1=ALU.mult), token=True)
            t_tr2 = pb.emit("tensor", lambda E: E.transpose(
                ps_t2[:, :], c8[:, :], ident[:, :]), deps=[t_c8, prev_use], token=True)
            comb_tok = pb.emit("vector", lambda E, a=sl_c: E.tensor_copy(
                out=comb8[:, a], in_=ps_t2[0:8, :]), deps=[t_tr2], token=True)
            prev_use = comb_tok
        st["comb_tok"] = comb_tok

        prev_ev = None
        for dc in range(8):
            t_sh = pb.emit("tensor", lambda E, dc=dc: E.matmul(
                ps_bsh[:, :], b2sh[:, dc * 128:(dc + 1) * 128],
                st["c_ones_row"][:, :], start=True, stop=True),
                deps=[t_b2sh, st.get("tok_ones_row"), prev_ev], token=True)
            for tt in range(NTT):
                sl_t = slice(tt * TTW, (tt + 1) * TTW)
                t1 = pb.emit("tensor", lambda E, dc=dc, a=sl_t: E.matmul(
                    ps_b2[:, :], b2sp[:, dc * 128:(dc + 1) * 128], comb8[:, a],
                    start=True, stop=True),
                    deps=[comb_tok, t_b2sp, prev_ev], token=True)
                pb.emit("vector", lambda E, dc=dc, tt=tt: E.scalar_tensor_tensor(
                    out=stream[:, dc * TPC + tt * TTW: dc * TPC + (tt + 1) * TTW],
                    in0=ps_bsh[:, :], scalar=0.5,
                    in1=stream[:, dc * TPC + tt * TTW: dc * TPC + (tt + 1) * TTW],
                    op0=ALU.mult, op1=ALU.add), deps=[t_sh])
                prev_ev = pb.emit("vector", lambda E, dc=dc, tt=tt: E.tensor_tensor(
                    out=stream[:, dc * TPC + tt * TTW: dc * TPC + (tt + 1) * TTW],
                    in0=stream[:, dc * TPC + tt * TTW: dc * TPC + (tt + 1) * TTW],
                    in1=ps_b2[:, :], op=ALU.add), deps=[t1], token=True)
        pb.build()


HSZ = 8


def run_experts(env, st, experts, tag):
    nc = env.nc
    max_nff = max(cfg["nff"] for cfg in experts)
    RING = 3
    with ExitStack() as es:
        w1sb = es.enter_context(nc.sbuf_tensor(tag + "_w1", [128, KC * max_nff * 128], BF16))
        w2sb = es.enter_context(nc.sbuf_tensor(tag + "_w2", [128, RING * 2048], BF16))
        hbuf = es.enter_context(nc.sbuf_tensor(tag + "_h", [128, HSZ * TTW], BF16))
        combB = es.enter_context(nc.sbuf_tensor(tag + "_cb", [128, TPC], F32))
        ps = [es.enter_context(nc.psum_tensor("ps%s_%d" % (tag, i), [128, TTW], F32))
              for i in range(8)]
        pb = PB(env, tag)
        xln, stream = st["xln"], st["stream"]

        prev = {"evac": {}, "w1_free": None, "w2_idx": 0, "w2_units": {},
                "w2_use": {}}
        for cfg in experts:
            nff = cfg["nff"]
            nffcols = nff * 128
            nhalf = (nff + HSZ - 1) // HSZ

            w1_toks = []
            for g in range(4):
                rows = slice(g * 256, (g + 1) * 256)
                def ld(E, rows=rows, g=g, src=cfg["w1_ap"], n=nffcols):
                    s = src[rows, :].rearrange("(two p) f -> p two f", p=128)
                    o = w1sb[:, g * 2 * n:(g + 1) * 2 * n].rearrange(
                        "p (two f) -> p two f", two=2)
                    return E.dma_start(out=o, in_=s)
                w1_toks.append(pb.emit("gpsimd", ld, deps=[prev.get("w1_free")],
                                       dma=True, dsem="w1g%d" % g))
            if cfg.get("gamma_sb") is not None:
                gts = []
                for kc in range(KC):
                    sl = slice(kc * nffcols, (kc + 1) * nffcols)
                    gts.append(pb.emit("vector", lambda E, a=sl, kc=kc, cfg=cfg: E.tensor_scalar(
                        out=w1sb[:, a], in0=w1sb[:, a],
                        scalar1=cfg["gamma_sb"][:, kc:kc + 1], scalar2=None,
                        op0=ALU.mult), deps=list(w1_toks) + [cfg.get("gamma_tok")],
                        token=True))
                w1_ready = gts
            else:
                w1_ready = w1_toks

            gelu_tok = {}
            cb_toks = None
            last_h = None
            for tt in range(NTT):
                sl_t = slice(tt * TTW, (tt + 1) * TTW)
                for half in range(nhalf):
                    f0 = half * HSZ
                    f1 = min(nff, f0 + HSZ)
                    for fl, ffc in enumerate(range(f0, f1)):
                        bank = fl % 2
                        deps = list(w1_ready)
                        if fl >= 2:
                            deps.append(gelu_tok[(tt, half, fl - 2)])
                        else:
                            deps.append(prev["evac"].get(bank))
                        deps.append(st["tok_xln"][tt])
                        for kc in range(KC):
                            tok = pb.emit("tensor",
                                          lambda E, kc=kc, ffc=ffc, bank=bank, tt=tt,
                                          n=nffcols, s=(kc == 0), p=(kc == KC - 1): E.matmul(
                                ps[bank][:, :],
                                w1sb[:, kc * n + ffc * 128: kc * n + (ffc + 1) * 128],
                                xln[:, kc * TPC + tt * TTW: kc * TPC + (tt + 1) * TTW],
                                start=s, stop=p),
                                deps=(deps if kc == 0 else ()), token=(kc == KC - 1))
                        last_h = tok
                        gelu_tok[(tt, half, fl)] = pb.emit(
                            "scalar", lambda E, fl=fl, ffc=ffc, bank=bank, cfg=cfg: E.activation(
                                out=hbuf[:, fl * TTW:(fl + 1) * TTW], in_=ps[bank][:, :],
                                func=cfg["gelu"],
                                bias=cfg["b1_sb"][:, cfg["b1_col0"] + ffc:
                                                  cfg["b1_col0"] + ffc + 1]),
                            deps=[tok, cfg.get("b1_tok")], token=True)
                    if cb_toks is None:
                        if cfg.get("const_scale"):
                            t_ms = pb.emit("vector", lambda E: E.memset(combB[:, :], 0.5),
                                           token=True)
                            cb_toks = [t_ms] * 4
                        else:
                            cb_toks = []
                            for j in range(4):
                                slc = slice(j * TTW, (j + 1) * TTW)
                                tmm = pb.emit("tensor", lambda E, a=slc, j=j, cfg=cfg: E.matmul(
                                    ps[4 + j][:, :], cfg["cb_lhsT"],
                                    cfg["cb_rhs"][:, a], start=True, stop=True),
                                    deps=[cfg.get("comb_tok"), prev["evac"].get(4 + j),
                                          st.get("tok_onehot8")],
                                    token=True)
                                cb_toks.append(pb.emit("vector",
                                                       lambda E, a=slc, j=j: E.tensor_copy(
                                    out=combB[:, a], in_=ps[4 + j][:, :]),
                                    deps=[tmm], token=True))
                    ffn2_tok = None
                    for fl, ffc in enumerate(range(f0, f1)):
                        if ffc % 2 == 0:
                            idx = prev["w2_idx"]
                            slot = idx % RING
                            rows = slice(ffc * 128, (ffc + 2) * 128)
                            def ldw2(E, rows=rows, slot=slot, src=cfg["w2_ap"]):
                                o = w2sb[:, slot * 2048: slot * 2048 + 2048].rearrange(
                                    "p (two f) -> p two f", two=2)
                                return E.dma_start(
                                    out=o,
                                    in_=src[rows, :].rearrange("(two p) f -> p two f", p=128))
                            tokw = pb.emit("gpsimd", ldw2,
                                           deps=[prev["w2_use"].get(idx - RING)],
                                           dma=True, dsem="w2s%d" % slot)
                            prev["w2_units"][idx] = (tokw, slot)
                            prev["w2_idx"] += 1
                        unit_i = prev["w2_idx"] - 1
                        w2tok, slot = prev["w2_units"][unit_i]
                        off = (ffc % 2) * 1024
                        for dc in range(8):
                            deps = [gelu_tok[(tt, half, fl)], w2tok]
                            if fl == 0:
                                deps.append(prev["evac"].get(dc))
                                deps.append(gelu_tok[(tt, half, f1 - f0 - 1)])
                                if dc >= 4:
                                    deps.append(cb_toks[dc - 4])
                            ffn2_tok = pb.emit("tensor",
                                               lambda E, dc=dc, slot=slot, off=off,
                                               s=(fl == 0), p=(fl == f1 - f0 - 1),
                                               fl=fl: E.matmul(
                                ps[dc][:, :],
                                w2sb[:, slot * 2048 + off + dc * 128:
                                      slot * 2048 + off + (dc + 1) * 128],
                                hbuf[:, fl * TTW:(fl + 1) * TTW],
                                start=s, stop=p),
                                deps=deps, token=(dc == 7))
                        prev["w2_use"][unit_i] = ffn2_tok
                    for dc in range(8):
                        pb.emit("vector", lambda E, dc=dc, a=sl_t: E.tensor_tensor(
                            out=ps[dc][:, :], in0=ps[dc][:, :], in1=combB[:, a],
                            op=ALU.mult),
                            deps=[ffn2_tok, cb_toks[tt]])
                        prev["evac"][dc] = pb.emit("vector",
                                                   lambda E, dc=dc, tt=tt: E.tensor_tensor(
                            out=stream[:, dc * TPC + tt * TTW: dc * TPC + (tt + 1) * TTW],
                            in0=stream[:, dc * TPC + tt * TTW: dc * TPC + (tt + 1) * TTW],
                            in1=ps[dc][:, :], op=ALU.add), token=True)
            prev["w1_free"] = last_h
        pb.build()


def phase_l0ffn(env, st, dram):
    with ExitStack() as es:
        st["m_sb"] = es.enter_context(env.nc.sbuf_tensor("m_sb", [1, TPC], F32))
        st["rstd_sb"] = es.enter_context(env.nc.sbuf_tensor("rstd_sb", [1, TPC], F32))
        run_ln(env, st, rms=False, tag="ln0")
        run_router(env, st, dram)
    gelu = AF.Gelu_apprx_tanh
    experts = [dict(nff=32, gelu=gelu, w1_ap=dram["l0_w1s"][0], w2_ap=dram["l0_w2s"][0],
                    b1_sb=st["c_l0_b1s"], b1_col0=0, b1_tok=st.get("tok_b1s"),
                    const_scale=True)]
    for e in range(8):
        experts.append(dict(
            nff=32, gelu=gelu,
            w1_ap=dram["l0_w1s"][e + 1], w2_ap=dram["l0_w2s"][e + 1],
            b1_sb=st["c_l0_b1s"], b1_col0=(e + 1) * 32, b1_tok=st.get("tok_b1s"),
            cb_lhsT=st["c_onehot8"][:, e * 128:(e + 1) * 128],
            cb_rhs=st["comb8"], comb_tok=st["comb_tok"]))
    run_experts(env, st, experts, "moe")


def build_l0ffn_program():
    nc = bass.Bass("TRN2", target_bir_lowering=False)
    es = ExitStack()
    env = Env(nc, es)
    wnames = ["l0_w1s", "l0_w2s", "l0_b1s", "l0_b2sp", "l0_b2sh",
              "l0_rwT", "l0_rcs", "l0_rb"]
    cnames = CONST_BASE
    with es:
        dram = declare_io(nc, wnames, cnames)
        st = {}
        st["stream"] = env.sbuf("stream", [128, KC * TPC], F32)
        st["xln"] = env.sbuf("xln", [128, KC * TPC], BF16)
        st["comb8"] = env.sbuf("comb8", [8, TPC], F32)
        st["c_l0_b1s"] = env.sbuf("c_l0_b1s", [128, 288], F32)

        pb0 = PB(env, "boot")
        reg_scalar_consts(env, pb0)
        load_consts(env, pb0, st, dram, cnames)
        st["tok_b1s"] = pb0.emit("sync", lambda E: E.dma_start(
            out=st["c_l0_b1s"][:, :], in_=dram["l0_b1s"][:, :]),
            dma=True, dsem="c_l0_b1s")
        emit_stream_in(env, pb0, st, dram)
        pb0.build()

        phase_l0ffn(env, st, dram)

        pbE = PB(env, "out")
        emit_stream_out(env, pbE, st, dram)
        pbE.build()
    nc.finalize()
    return nc


def _in_map_base(w, consts, wnames, cnames, x_slice):
    m = {nm: w[nm] for nm in wnames}
    for nm in cnames:
        m[nm] = consts[nm]
    m["x_in"] = x_slice
    return m


L0FFN_WNAMES = ["l0_w1s", "l0_w2s", "l0_b1s", "l0_b2sp", "l0_b2sh",
                "l0_rwT", "l0_rcs", "l0_rb"]


def _launch(nc, w, consts, wnames, cnames, xs):
    in_maps = [_in_map_base(w, consts[c], wnames, cnames, xs[c])
               for c in range(NCORES)]
    res = run_bass_kernel_spmd(nc, in_maps, core_ids=list(range(NCORES)))
    return [np.asarray(res.results[c]["x_out"]) for c in range(NCORES)]


def kernel(**inputs):
    w = _prep_weights(inputs)
    consts = [_host_consts(c) for c in range(NCORES)]
    x = np.ascontiguousarray(np.asarray(inputs["L0_attn_x"], np.float32).reshape(B * T, D))
    xs = [np.ascontiguousarray(x[c * TPC:(c + 1) * TPC, :].T) for c in range(NCORES)]

    attn_cn = ["ones128x1_f32", "ones128x1_bf", "ones1x128_f32"]
    xs = _launch(build_l0ffn_program(), w, consts, L0FFN_WNAMES, CONST_BASE, xs)
    xs = _launch(build_l0attn_program(), w, consts, L0ATTN_WNAMES, attn_cn, xs)
    xs = _launch(build_l0ffn_program(), w, consts, L0FFN_WNAMES, CONST_BASE, xs)
    xs = _launch(build_l1ffn_program("f1"), w, consts, L1FFN_WNAMES, L1FFN_CNAMES, xs)
    xs = _launch(build_l1attn_program(), w, consts, L1ATTN_WNAMES, L1ATTN_CNAMES, xs)
    xs = _launch(build_l1ffn_program("f2"), w, consts, L1FFN_WNAMES, L1FFN_CNAMES, xs)

    out = np.concatenate([s.T for s in xs], axis=0)
    return out.reshape(B, T, D)


def phase_l0attn(env, st, dram):
    nc = env.nc
    with ExitStack() as es:
        st["m_sb"] = es.enter_context(nc.sbuf_tensor("m_sb", [1, TPC], F32))
        st["rstd_sb"] = es.enter_context(nc.sbuf_tensor("rstd_sb", [1, TPC], F32))
        run_ln(env, st, rms=False, tag="lna")
    xln, stream = st["xln"], st["stream"]

    with ExitStack() as es:
        wsl = es.enter_context(nc.sbuf_tensor("a_wsl", [128, 8 * 1024], BF16))
        ktm = es.enter_context(nc.sbuf_tensor("a_ktm", [128, 16 * 1024], BF16))
        vtm = es.enter_context(nc.sbuf_tensor("a_vtm", [128, 16 * 1024], BF16))
        kvt = es.enter_context(nc.sbuf_tensor("a_kvt", [128, 2 * TTW], F32))
        ps_kv = [es.enter_context(nc.psum_tensor("a_pskv%d" % i, [128, TTW], F32))
                 for i in range(4)]
        pb = PB(env, "l0kv")

        def load_w(nm, dep):
            toks = []
            for kc in range(KC):
                toks.append(pb.emit("gpsimd", lambda E, kc=kc, nm=nm: E.dma_start(
                    out=wsl[:, kc * 1024:(kc + 1) * 1024],
                    in_=dram[nm][kc * 128:(kc + 1) * 128, :]),
                    deps=[dep], dma=True, dsem="aw%d" % kc))
            return toks

        def project_tm(dst, wtoks, extra_dep=None):
            evs = []
            last_mm = None
            for tc in range(16):
                for jh in range(2):
                    bank = (tc * 2 + jh) % 2
                    tok = None
                    for kc in range(KC):
                        tok = pb.emit("tensor", lambda E, kc=kc, tc=tc, jh=jh, bank=bank,
                                      s=(kc == 0), p=(kc == KC - 1): E.matmul(
                            ps_kv[bank][:, :],
                            xln[:, kc * TPC + tc * 128: kc * TPC + (tc + 1) * 128],
                            wsl[:, kc * 1024 + jh * 512: kc * 1024 + (jh + 1) * 512],
                            start=s, stop=p),
                            deps=(wtoks + [st["tok_xln"][(tc * 128) // TTW],
                                           evs[-2] if len(evs) >= 2 else None,
                                           extra_dep] if kc == 0 else ()),
                            token=(kc == KC - 1))
                    last_mm = tok
                    evs.append(pb.emit("vector", lambda E, tc=tc, jh=jh, bank=bank: E.tensor_copy(
                        out=dst[:, tc * 1024 + jh * 512: tc * 1024 + (jh + 1) * 512],
                        in_=ps_kv[bank][:, :]), deps=[tok], token=True))
            return evs, last_mm

        wk = load_w("l0_w_k", None)
        ev_k, last_k = project_tm(ktm, wk)
        wv = load_w("l0_w_v", last_k)
        ev_v, last_v = project_tm(vtm, wv, extra_dep=None)

        st_toks = []
        prev_ev = [None, None]
        last_store = {0: None, 1: None}
        for ic in range(8):
            for jh in range(2):
                bank = 2 + ((ic * 2 + jh) % 2)
                tok = None
                for tc in range(16):
                    tok = pb.emit("tensor", lambda E, tc=tc, ic=ic, jh=jh, bank=bank,
                                  s=(tc == 0), p=(tc == 15): E.matmul(
                        ps_kv[bank][:, :],
                        ktm[:, tc * 1024 + ic * 128: tc * 1024 + (ic + 1) * 128],
                        vtm[:, tc * 1024 + jh * 512: tc * 1024 + (jh + 1) * 512],
                        start=s, stop=p),
                        deps=([ev_k[-1], ev_v[-1], prev_ev[(ic * 2 + jh) % 2]]
                              if tc == 0 else ()),
                        token=(tc == 15))
                t_ev = pb.emit("vector", lambda E, jh=jh, bank=bank: E.tensor_copy(
                    out=kvt[:, jh * TTW:(jh + 1) * TTW], in_=ps_kv[bank][:, :]),
                    deps=[tok, last_store[jh]], token=True)
                prev_ev[(ic * 2 + jh) % 2] = t_ev
                t_st = pb.emit("sync", lambda E, ic=ic, jh=jh: E.dma_start(
                    out=dram["cc_kv_in"][ic * 128:(ic + 1) * 128, jh * TTW:(jh + 1) * TTW],
                    in_=kvt[:, jh * TTW:(jh + 1) * TTW]),
                    deps=[t_ev, last_store[jh]], dma=True, dsem="kvo%d" % jh)
                last_store[jh] = t_st
                st_toks.append(t_st)
        st["tok_ar"] = pb.emit("gpsimd", lambda E: E.collective_compute(
            "AllReduce", ALU.add, replica_groups=[[0, 1], [2, 3], [4, 5], [6, 7]],
            ins=[dram["cc_kv_in"].ap().opt()], outs=[dram["cc_kv_out"].ap().opt()]),
            deps=st_toks, token=True)
        pb.build()

    with ExitStack() as es:
        wsl = es.enter_context(nc.sbuf_tensor("b_wsl", [128, 8 * 1024], BF16))
        qfm = es.enter_context(nc.sbuf_tensor("b_qfm", [128, 8 * TPC], BF16))
        msb = es.enter_context(nc.sbuf_tensor("b_msb", [128, 8 * 1024], BF16))
        mscr = es.enter_context(nc.sbuf_tensor("b_mscr", [128, 2 * 1024], F32))
        ps_q = [es.enter_context(nc.psum_tensor("b_psq%d" % i, [128, TTW], F32))
                for i in range(4)]
        pb = PB(env, "l0qm")

        wq = []
        for kc in range(KC):
            wq.append(pb.emit("gpsimd", lambda E, kc=kc: E.dma_start(
                out=wsl[:, kc * 1024:(kc + 1) * 1024],
                in_=dram["l0_w_q"][kc * 128:(kc + 1) * 128, :]),
                dma=True, dsem="aw%d" % kc))
        ev_prev = [None, None]
        q_ev = []
        for jc in range(8):
            for tt in range(NTT):
                bank = (jc * NTT + tt) % 2
                tok = None
                for kc in range(KC):
                    tok = pb.emit("tensor", lambda E, kc=kc, jc=jc, tt=tt, bank=bank,
                                  s=(kc == 0), p=(kc == KC - 1): E.matmul(
                        ps_q[bank][:, :],
                        wsl[:, kc * 1024 + jc * 128: kc * 1024 + (jc + 1) * 128],
                        xln[:, kc * TPC + tt * TTW: kc * TPC + (tt + 1) * TTW],
                        start=s, stop=p),
                        deps=(wq + [st["tok_xln"][tt], ev_prev[bank]] if kc == 0 else ()),
                        token=(kc == KC - 1))
                t_ev = pb.emit("vector", lambda E, jc=jc, tt=tt, bank=bank: E.tensor_copy(
                    out=qfm[:, jc * TPC + tt * TTW: jc * TPC + (tt + 1) * TTW],
                    in_=ps_q[bank][:, :]), deps=[tok], token=True)
                ev_prev[bank] = t_ev
                q_ev.append(t_ev)
        m_toks = []
        prev_sc = [None, None]
        for ic in range(8):
            t_ld = pb.emit("sync", lambda E, ic=ic: E.dma_start(
                out=mscr[:, (ic % 2) * 1024:(ic % 2 + 1) * 1024],
                in_=dram["cc_kv_out"][ic * 128:(ic + 1) * 128, :]),
                deps=[st["tok_ar"], prev_sc[ic % 2]], dma=True, dsem="mld%d" % (ic % 2))
            t_sc = pb.emit("vector", lambda E, ic=ic: E.tensor_scalar(
                out=msb[:, ic * 1024:(ic + 1) * 1024],
                in0=mscr[:, (ic % 2) * 1024:(ic % 2 + 1) * 1024],
                scalar1=SQRTD, scalar2=None, op0=ALU.mult),
                deps=[t_ld], token=True)
            prev_sc[ic % 2] = t_sc
            m_toks.append(t_sc)
        prev_y = [None, None]
        for jc in range(8):
            for tt in range(NTT):
                bank = 2 + (jc * NTT + tt) % 2
                tok = None
                for ic in range(8):
                    tok = pb.emit("tensor", lambda E, ic=ic, jc=jc, tt=tt, bank=bank,
                                  s=(ic == 0), p=(ic == KC - 1): E.matmul(
                        ps_q[bank][:, :],
                        msb[:, ic * 1024 + jc * 128: ic * 1024 + (jc + 1) * 128],
                        qfm[:, ic * TPC + tt * TTW: ic * TPC + (tt + 1) * TTW],
                        start=s, stop=p),
                        deps=(m_toks + [q_ev[-1], prev_y[bank - 2]] if ic == 0 else ()),
                        token=(ic == KC - 1))
                prev_y[bank - 2] = pb.emit("vector", lambda E, jc=jc, tt=tt, bank=bank: E.tensor_tensor(
                    out=stream[:, jc * TPC + tt * TTW: jc * TPC + (tt + 1) * TTW],
                    in0=stream[:, jc * TPC + tt * TTW: jc * TPC + (tt + 1) * TTW],
                    in1=ps_q[bank][:, :], op=ALU.add), deps=[tok], token=True)
        pb.build()


L0ATTN_WNAMES = ["l0_w_q", "l0_w_k", "l0_w_v"]


def build_l0attn_program():
    nc = bass.Bass("TRN2", target_bir_lowering=False)
    es = ExitStack()
    env = Env(nc, es)
    cnames = ["ones128x1_f32", "ones128x1_bf", "ones1x128_f32"]
    with es:
        dram = declare_io(nc, L0ATTN_WNAMES, cnames,
                          collectives=[("cc_kv_in", [1024, 1024]),
                                       ("cc_kv_out", [1024, 1024])])
        st = {}
        st["stream"] = env.sbuf("stream", [128, KC * TPC], F32)
        st["xln"] = env.sbuf("xln", [128, KC * TPC], BF16)

        pb0 = PB(env, "boot")
        reg_scalar_consts(env, pb0)
        load_consts(env, pb0, st, dram, cnames)
        emit_stream_in(env, pb0, st, dram)
        pb0.build()

        phase_l0attn(env, st, dram)

        pbE = PB(env, "out")
        emit_stream_out(env, pbE, st, dram)
        pbE.build()
    nc.finalize()
    return nc


def phase_l1ffn(env, st, dram, tag):
    nc = env.nc
    with ExitStack() as es:
        maskh = es.enter_context(nc.sbuf_tensor(tag + "maskh", [1, TPC], F32))

        with ExitStack() as esm:
          st["m_sb"] = esm.enter_context(nc.sbuf_tensor(tag + "m_sb", [1, TPC], F32))
          st["rstd_sb"] = esm.enter_context(nc.sbuf_tensor(tag + "rstd_sb", [1, TPC], F32))
          smy = esm.enter_context(nc.sbuf_tensor(tag + "smy", [1, TPC], F32))
          b2h = esm.enter_context(nc.sbuf_tensor(tag + "b2h", [1, 1024], F32))
          run_ln(env, st, rms=True, tag=tag + "ln")
          if True:
            es2 = esm.enter_context(ExitStack())
            rt = es2.enter_context(nc.sbuf_tensor(tag + "rt", [128, 8], F32))
            b2l = es2.enter_context(nc.sbuf_tensor(tag + "b2l", [1, 1024], F32))
            sv1 = es2.enter_context(nc.sbuf_tensor(tag + "sv1", [1, TPC], F32))
            ps_s = es2.enter_context(nc.psum_tensor(tag + "ps_s", [1, TPC], F32))
            ps_sr = es2.enter_context(nc.psum_tensor(tag + "ps_sr", [1, 1], F32))
            ps_b = es2.enter_context(nc.psum_tensor(tag + "ps_b", [128, TTW], F32))
            pb = PB(env, tag + "s")
            t_rt = pb.emit("sync", lambda E: E.dma_start(out=rt[:, :], in_=dram["l1_rt"][:, :]),
                           dma=True, dsem="c_l1rt")
            t_b2l = pb.emit("sync", lambda E: E.dma_start(out=b2l[:, :], in_=dram["l1_b2l"][:, :]),
                            dma=True, dsem="c_l1b2l")
            st[tag + "_tok_b2h"] = pb.emit("sync", lambda E: E.dma_start(
                out=b2h[:, :], in_=dram["l1_b2h"][:, :]), dma=True, dsem="c_l1b2h")
            t_mm = None
            for tt in range(NTT):
                for kc in range(KC):
                    t_mm = pb.emit("tensor", lambda E, kc=kc, tt=tt, s=(kc == 0),
                                   p=(kc == KC - 1): E.matmul(
                        ps_s[:, tt * TTW:(tt + 1) * TTW], rt[:, kc:kc + 1],
                        st["stream"][:, kc * TPC + tt * TTW: kc * TPC + (tt + 1) * TTW],
                        start=s, stop=p), deps=[t_rt] if kc == 0 else (),
                        token=(kc == KC - 1 and tt == NTT - 1))
            t_sr = None
            for kc in range(KC):
                t_sr = pb.emit("tensor", lambda E, kc=kc, s=(kc == 0), p=(kc == KC - 1): E.matmul(
                    ps_sr[:, :], rt[:, kc:kc + 1], st["c_ones128x1_f32"][:, :],
                    start=s, stop=p),
                    deps=[t_rt, st.get("tok_ones128x1_f32")] if kc == 0 else (),
                    token=(kc == KC - 1))
            t1 = pb.emit("vector", lambda E: E.tensor_scalar(
                out=sv1[:, :], in0=st["m_sb"][:, :], scalar1=ps_sr[0:1, 0:1],
                scalar2=None, op0=ALU.mult), deps=[t_sr])
            t2 = pb.emit("vector", lambda E: E.tensor_tensor(
                out=sv1[:, :], in0=ps_s[:, :], in1=sv1[:, :], op=ALU.subtract),
                deps=[t_mm])
            t_smy = pb.emit("vector", lambda E: E.tensor_tensor(
                out=smy[:, :], in0=sv1[:, :], in1=st["rstd_sb"][:, :], op=ALU.mult),
                token=True)
            t_sst = pb.emit("sync", lambda E: E.dma_start(
                out=dram[tag + "cc_s_in"][:, :], in_=smy[:, :]),
                deps=[t_smy], dma=True, dsem="sso")
            st["tok_ag"] = pb.emit("gpsimd", lambda E: E.collective_compute(
                "AllGather", ALU.bypass, replica_groups=[[0, 1], [2, 3], [4, 5], [6, 7]],
                ins=[dram[tag + "cc_s_in"].ap().opt()],
                outs=[dram[tag + "cc_s_out"].ap().opt()]),
                deps=[t_sst], token=True)
            prev = None
            for dc in range(8):
                t_o = pb.emit("tensor", lambda E, dc=dc: E.matmul(
                    ps_b[:, :], b2l[:, dc * 128:(dc + 1) * 128],
                    st["c_ones_row"][:, :], start=True, stop=True),
                    deps=[t_b2l, st.get("tok_ones_row"), prev], token=True)
                for tt in range(NTT):
                    prev = pb.emit("vector", lambda E, dc=dc, tt=tt: E.scalar_tensor_tensor(
                        out=st["stream"][:, dc * TPC + tt * TTW: dc * TPC + (tt + 1) * TTW],
                        in0=ps_b[:, :], scalar=0.5,
                        in1=st["stream"][:, dc * TPC + tt * TTW: dc * TPC + (tt + 1) * TTW],
                        op0=ALU.mult, op1=ALU.add), deps=[t_o], token=True)
            pb.build()
            es2.close()

            es2 = esm.enter_context(ExitStack())
            srow = es2.enter_context(nc.sbuf_tensor(tag + "srow", [128, 32], F32))
            scrow = es2.enter_context(nc.sbuf_tensor(tag + "scrow", [128, 32], F32))
            urow = es2.enter_context(nc.sbuf_tensor(tag + "urow", [128, 32], F32))
            erow = es2.enter_context(nc.sbuf_tensor(tag + "erow", [128, 32], F32))
            tierow = es2.enter_context(nc.sbuf_tensor(tag + "tierow", [128, 32], F32))
            red = es2.enter_context(nc.sbuf_tensor(tag + "red", [128, 1], F32))
            bc1 = es2.enter_context(nc.sbuf_tensor(tag + "bc1", [128, 1], F32))
            scr = es2.enter_context(nc.sbuf_tensor(tag + "scr", [1, 16], F32))
            umy = es2.enter_context(nc.sbuf_tensor(tag + "umy", [1, TPC], F32))
            scmy = es2.enter_context(nc.sbuf_tensor(tag + "scmy", [1, TPC], F32))
            mv1 = es2.enter_context(nc.sbuf_tensor(tag + "mv1", [1, TPC], F32))
            mv2 = es2.enter_context(nc.sbuf_tensor(tag + "mv2", [1, TPC], F32))
            ps_b1 = es2.enter_context(nc.psum_tensor(tag + "ps_b1", [128, 1], F32))
            ps_sum = es2.enter_context(nc.psum_tensor(tag + "ps_sum", [1, 1], F32))
            ps_bo = es2.enter_context(nc.psum_tensor(tag + "ps_bo", [128, TTW], F32))
            pb = PB(env, tag + "r")
            ones1r = st["c_ones1x128_f32"]
            ones_c = st["c_ones128x1_f32"]

            sr_ap = dram[tag + "cc_s_out"].ap().rearrange("h (pp f) -> (h pp) f", pp=64)
            t_srow = pb.emit("sync", lambda E: E.dma_start(out=srow[:, :], in_=sr_ap),
                             deps=[st["tok_ag"]], dma=True, dsem="ssl")
            CC = float(EPS_CD * np.log(8.0))
            t_i1 = pb.emit("vector", lambda E: E.memset(scr[:, 0:1], 0.0))
            t_i2 = pb.emit("vector", lambda E: E.memset(scr[:, 1:2], CC))

            war = {"b1": None, "sum": None}

            def bcast(col, deps):
                tmm = pb.emit("tensor", lambda E, col=col: E.matmul(
                    ps_b1[:, :], ones1r[:, :], scr[:, col:col + 1], start=True, stop=True),
                    deps=deps + [st.get("tok_ones1x128_f32"), war["b1"]], token=True)
                t = pb.emit("vector", lambda E: E.tensor_copy(
                    out=bc1[:, :], in_=ps_b1[:, :]), deps=[tmm], token=True)
                war["b1"] = t
                return t

            def rowsum(src_fn, deps):
                t_r = pb.emit("vector", src_fn, deps=deps, token=True)
                t = pb.emit("tensor", lambda E: E.matmul(
                    ps_sum[:, :], red[:, :], ones_c[:, :], start=True, stop=True),
                    deps=[t_r, st.get("tok_ones128x1_f32"), war["sum"]], token=True)
                return t

            t_a = t_i1
            for it in range(CD_ITERS):
                t_bc = bcast(0, [t_a, t_srow])
                t_u = pb.emit("vector", lambda E: E.tensor_scalar(
                    out=urow[:, :], in0=srow[:, :], scalar1=bc1[:, 0:1], scalar2=0.0,
                    op0=ALU.add, op1=ALU.min), deps=[t_bc], token=True)
                t_e = pb.emit("scalar", lambda E: E.activation(
                    out=erow[:, :], in_=urow[:, :], func=AF.Exp, scale=10.0),
                    deps=[t_u], token=True)
                t_sum = rowsum(lambda E: E.reduce_sum(red[:, :], erow[:, :], axis=AX.X),
                               [t_e])
                t_ln = pb.emit("scalar", lambda E: E.activation(
                    out=scr[:, 5:6], in_=ps_sum[0:1, 0:1], func=AF.Ln),
                    deps=[t_sum], token=True)
                war["sum"] = t_ln
                t_a = pb.emit("vector", lambda E: E.tensor_scalar(
                    out=scr[:, 0:1], in0=scr[:, 5:6], scalar1=-EPS_CD,
                    scalar2=scr[:, 1:2], op0=ALU.mult, op1=ALU.add),
                    deps=[t_ln, t_i2], token=True)
                t_i2 = pb.emit("vector", lambda E: E.tensor_scalar(
                    out=scr[:, 1:2], in0=scr[:, 0:1], scalar1=CC, scalar2=None,
                    op0=ALU.add), token=True)
            t_bc = bcast(0, [t_a])
            t_u = pb.emit("vector", lambda E: E.tensor_scalar(
                out=urow[:, :], in0=srow[:, :], scalar1=bc1[:, 0:1], scalar2=0.0,
                op0=ALU.add, op1=ALU.min), deps=[t_bc], token=True)
            t_sc = pb.emit("scalar", lambda E: E.activation(
                out=scrow[:, :], in_=urow[:, :], func=AF.Exp, scale=10.0),
                deps=[t_u], token=True)
            t_umy = pb.emit("vector", lambda E: E.tensor_scalar(
                out=umy[:, :], in0=smy[:, :],
                scalar1=scr[:, 0:1], scalar2=0.0,
                op0=ALU.add, op1=ALU.min), deps=[t_a], token=True)
            t_scmy = pb.emit("scalar", lambda E: E.activation(
                out=scmy[:, :], in_=umy[:, :], func=AF.Exp, scale=10.0),
                deps=[t_umy], token=True)

            t_lo = pb.emit("vector", lambda E: E.memset(scr[:, 2:3], -120.0))
            t_hi = pb.emit("vector", lambda E: E.memset(scr[:, 3:4], 1e-3))
            t_upd = None
            for it in range(TH_ITERS):
                t_m1 = pb.emit("vector", lambda E: E.tensor_tensor(
                    out=scr[:, 4:5], in0=scr[:, 2:3], in1=scr[:, 3:4], op=ALU.add),
                    deps=[t_lo, t_hi, t_upd], token=True)
                t_m2 = pb.emit("vector", lambda E: E.tensor_scalar(
                    out=scr[:, 4:5], in0=scr[:, 4:5], scalar1=0.5, scalar2=None,
                    op0=ALU.mult), token=True)
                t_ms = pb.emit("scalar", lambda E: E.activation(
                    out=scr[:, 5:6], in_=scr[:, 4:5], func=AF.Exp, scale=10.0),
                    deps=[t_m2], token=True)
                t_bc = bcast(5, [t_ms])
                t_cnt = rowsum(lambda E: E.reduce_sum(
                    red[:, :], erow[:, :], axis=AX.X), [
                    pb.emit("vector", lambda E: E.tensor_scalar(
                        out=erow[:, :], in0=scrow[:, :], scalar1=bc1[:, 0:1],
                        scalar2=None, op0=ALU.is_ge), deps=[t_bc, t_sc], token=True)])
                t_c = pb.emit("vector", lambda E: E.tensor_scalar(
                    out=scr[:, 12:13], in0=ps_sum[0:1, 0:1], scalar1=1024.0,
                    scalar2=None, op0=ALU.is_ge), deps=[t_cnt], token=True)
                war["sum"] = t_c
                pb.emit("vector", lambda E: E.tensor_tensor(
                    out=scr[:, 5:6], in0=scr[:, 4:5], in1=scr[:, 2:3], op=ALU.subtract))
                pb.emit("vector", lambda E: E.tensor_tensor(
                    out=scr[:, 5:6], in0=scr[:, 5:6], in1=scr[:, 12:13], op=ALU.mult))
                t_lo = pb.emit("vector", lambda E: E.tensor_tensor(
                    out=scr[:, 2:3], in0=scr[:, 2:3], in1=scr[:, 5:6], op=ALU.add),
                    token=True)
                pb.emit("vector", lambda E: E.tensor_scalar(
                    out=scr[:, 13:14], in0=scr[:, 12:13], scalar1=-1.0, scalar2=1.0,
                    op0=ALU.mult, op1=ALU.add))
                pb.emit("vector", lambda E: E.tensor_tensor(
                    out=scr[:, 5:6], in0=scr[:, 4:5], in1=scr[:, 3:4], op=ALU.subtract))
                pb.emit("vector", lambda E: E.tensor_tensor(
                    out=scr[:, 5:6], in0=scr[:, 5:6], in1=scr[:, 13:14], op=ALU.mult))
                t_upd = pb.emit("vector", lambda E: E.tensor_tensor(
                    out=scr[:, 3:4], in0=scr[:, 3:4], in1=scr[:, 5:6], op=ALU.add),
                    token=True)
                t_hi = t_upd
            t_sl = pb.emit("scalar", lambda E: E.activation(
                out=scr[:, 6:7], in_=scr[:, 2:3], func=AF.Exp, scale=10.0),
                deps=[t_lo], token=True)
            t_sh = pb.emit("scalar", lambda E: E.activation(
                out=scr[:, 7:8], in_=scr[:, 3:4], func=AF.Exp, scale=10.0),
                deps=[t_upd], token=True)
            t_bc = bcast(7, [t_sh])
            t_cnt = rowsum(lambda E: E.reduce_sum(red[:, :], erow[:, :], axis=AX.X), [
                pb.emit("vector", lambda E: E.tensor_scalar(
                    out=erow[:, :], in0=scrow[:, :], scalar1=bc1[:, 0:1],
                    scalar2=None, op0=ALU.is_ge), deps=[t_bc], token=True)])
            t_kr = pb.emit("vector", lambda E: E.tensor_scalar(
                out=scr[:, 8:9], in0=ps_sum[0:1, 0:1], scalar1=-1.0, scalar2=1024.0,
                op0=ALU.mult, op1=ALU.add), deps=[t_cnt], token=True)
            war["sum"] = t_kr
            t_bc = bcast(6, [t_sl])
            t_g1 = pb.emit("vector", lambda E: E.tensor_scalar(
                out=tierow[:, :], in0=scrow[:, :], scalar1=bc1[:, 0:1],
                scalar2=None, op0=ALU.is_ge), deps=[t_bc], token=True)
            t_bc2 = bcast(7, [t_sh])
            t_g2 = pb.emit("vector", lambda E: E.tensor_scalar(
                out=erow[:, :], in0=scrow[:, :], scalar1=bc1[:, 0:1],
                scalar2=None, op0=ALU.is_ge), deps=[t_bc2], token=True)
            t_tie = pb.emit("vector", lambda E: E.tensor_tensor(
                out=tierow[:, :], in0=tierow[:, :], in1=erow[:, :], op=ALU.subtract),
                token=True)
            t_ilo = pb.emit("vector", lambda E: E.memset(scr[:, 9:10], -1.0))
            t_ihi = pb.emit("vector", lambda E: E.memset(scr[:, 10:11], 4096.0))
            t_upd = None
            for it in range(IX_ITERS):
                t_m1 = pb.emit("vector", lambda E: E.tensor_tensor(
                    out=scr[:, 11:12], in0=scr[:, 9:10], in1=scr[:, 10:11], op=ALU.add),
                    deps=[t_ilo, t_ihi, t_upd], token=True)
                t_m2 = pb.emit("vector", lambda E: E.tensor_scalar(
                    out=scr[:, 11:12], in0=scr[:, 11:12], scalar1=0.5, scalar2=None,
                    op0=ALU.mult), token=True)
                t_bc = bcast(11, [t_m2])
                t_cmp = pb.emit("vector", lambda E: E.tensor_scalar(
                    out=erow[:, :], in0=st["c_idx_row"][:, :], scalar1=bc1[:, 0:1],
                    scalar2=None, op0=ALU.is_le),
                    deps=[t_bc, st.get("tok_idx_row")], token=True)
                t_and = pb.emit("vector", lambda E: E.tensor_tensor(
                    out=erow[:, :], in0=erow[:, :], in1=tierow[:, :], op=ALU.mult),
                    deps=[t_tie], token=True)
                t_cnt = rowsum(lambda E: E.reduce_sum(red[:, :], erow[:, :], axis=AX.X),
                               [t_and])
                t_c = pb.emit("vector", lambda E: E.tensor_tensor(
                    out=scr[:, 12:13], in0=ps_sum[0:1, 0:1], in1=scr[:, 8:9],
                    op=ALU.is_ge), deps=[t_cnt, t_kr], token=True)
                war["sum"] = t_c
                pb.emit("vector", lambda E: E.tensor_tensor(
                    out=scr[:, 5:6], in0=scr[:, 11:12], in1=scr[:, 10:11], op=ALU.subtract))
                pb.emit("vector", lambda E: E.tensor_tensor(
                    out=scr[:, 5:6], in0=scr[:, 5:6], in1=scr[:, 12:13], op=ALU.mult))
                t_ihi = pb.emit("vector", lambda E: E.tensor_tensor(
                    out=scr[:, 10:11], in0=scr[:, 10:11], in1=scr[:, 5:6], op=ALU.add),
                    token=True)
                pb.emit("vector", lambda E: E.tensor_scalar(
                    out=scr[:, 13:14], in0=scr[:, 12:13], scalar1=-1.0, scalar2=1.0,
                    op0=ALU.mult, op1=ALU.add))
                pb.emit("vector", lambda E: E.tensor_tensor(
                    out=scr[:, 5:6], in0=scr[:, 11:12], in1=scr[:, 9:10], op=ALU.subtract))
                pb.emit("vector", lambda E: E.tensor_tensor(
                    out=scr[:, 5:6], in0=scr[:, 5:6], in1=scr[:, 13:14], op=ALU.mult))
                t_upd = pb.emit("vector", lambda E: E.tensor_tensor(
                    out=scr[:, 9:10], in0=scr[:, 9:10], in1=scr[:, 5:6], op=ALU.add),
                    token=True)
                t_ilo = t_upd
            t_1 = pb.emit("vector", lambda E: E.tensor_scalar(
                out=mv1[:, :], in0=scmy[:, :], scalar1=scr[:, 7:8], scalar2=None,
                op0=ALU.is_ge), deps=[t_scmy, t_sh], token=True)
            t_2 = pb.emit("vector", lambda E: E.tensor_scalar(
                out=mv2[:, :], in0=scmy[:, :], scalar1=scr[:, 6:7], scalar2=None,
                op0=ALU.is_ge), deps=[t_sl], token=True)
            pb.emit("vector", lambda E: E.tensor_tensor(
                out=mv2[:, :], in0=mv2[:, :], in1=mv1[:, :], op=ALU.subtract))
            pb.emit("vector", lambda E: E.tensor_scalar(
                out=umy[:, :], in0=st["c_idxg"][:, :], scalar1=scr[:, 10:11],
                scalar2=None, op0=ALU.is_le), deps=[t_ihi, st.get("tok_idxg")])
            pb.emit("vector", lambda E: E.tensor_tensor(
                out=mv2[:, :], in0=mv2[:, :], in1=umy[:, :], op=ALU.mult))
            pb.emit("vector", lambda E: E.tensor_tensor(
                out=mv1[:, :], in0=mv1[:, :], in1=mv2[:, :], op=ALU.add))
            t_mask = pb.emit("vector", lambda E: E.tensor_scalar(
                out=maskh[:, :], in0=mv1[:, :], scalar1=0.5, scalar2=None,
                op0=ALU.mult), token=True)
            st["tok_mask"] = t_mask
            b2h2 = b2h
            prev = None
            for dc in range(8):
                for tt in range(NTT):
                    t_o = pb.emit("tensor", lambda E, dc=dc, tt=tt: E.matmul(
                        ps_bo[:, :], b2h2[:, dc * 128:(dc + 1) * 128],
                        maskh[:, tt * TTW:(tt + 1) * TTW], start=True, stop=True),
                        deps=[t_mask, st[tag + "_tok_b2h"], prev], token=True)
                    prev = pb.emit("vector", lambda E, dc=dc, tt=tt: E.tensor_tensor(
                        out=st["stream"][:, dc * TPC + tt * TTW: dc * TPC + (tt + 1) * TTW],
                        in0=st["stream"][:, dc * TPC + tt * TTW: dc * TPC + (tt + 1) * TTW],
                        in1=ps_bo[:, :], op=ALU.add), deps=[t_o], token=True)
            pb.build()

        experts = [
            dict(nff=4, gelu=AF.Gelu, w1_ap=dram["l1_w1l"], w2_ap=dram["l1_w2l"],
                 b1_sb=st["c_l1_b1l"], b1_col0=0, b1_tok=st.get("tok_l1_b1l"),
                 gamma_sb=st["c_l1_gl"], gamma_tok=st.get("tok_l1_gl"),
                 const_scale=True),
            dict(nff=32, gelu=AF.Gelu, w1_ap=dram["l1_w1h"], w2_ap=dram["l1_w2h"],
                 b1_sb=st["c_l1_b1h"], b1_col0=0, b1_tok=st.get("tok_l1_b1h"),
                 gamma_sb=st["c_l1_gh"], gamma_tok=st.get("tok_l1_gh"),
                 cb_lhsT=st["c_ones1x128_f32"][:, :], cb_rhs=maskh,
                 comb_tok=st["tok_mask"]),
        ]
        run_experts(env, st, experts, tag + "ex")


L1FFN_WNAMES = ["l1_w1l", "l1_w2l", "l1_w1h", "l1_w2h", "l1_b1l", "l1_b1h",
                "l1_b2l", "l1_b2h", "l1_gl", "l1_gh", "l1_rt"]
L1FFN_CNAMES = ["ones128x1_f32", "ones128x1_bf", "ones1x128_f32",
                "ones_row", "idxg", "idx_row"]


def build_l1ffn_program(tag="f1"):
    nc = bass.Bass("TRN2", target_bir_lowering=False)
    es = ExitStack()
    env = Env(nc, es)
    with es:
        dram = declare_io(nc, L1FFN_WNAMES, L1FFN_CNAMES,
                          collectives=[(tag + "cc_s_in", [1, TPC]),
                                       (tag + "cc_s_out", [2, TPC])])
        st = {}
        st["stream"] = env.sbuf("stream", [128, KC * TPC], F32)
        st["xln"] = env.sbuf("xln", [128, KC * TPC], BF16)
        for nm, shape in [("l1_b1l", [128, 4]), ("l1_b1h", [128, 32]),
                          ("l1_gl", [128, 8]), ("l1_gh", [128, 8])]:
            st["c_" + nm[3:] if False else "c_l1_" + nm[3:]] = env.sbuf("c_" + nm, shape, F32)

        pb0 = PB(env, "boot")
        reg_scalar_consts(env, pb0)
        load_consts(env, pb0, st, dram, L1FFN_CNAMES)
        for nm in ["l1_b1l", "l1_b1h", "l1_gl", "l1_gh"]:
            st["tok_" + nm] = pb0.emit("sync", lambda E, a=st["c_l1_" + nm[3:]],
                                       b=dram[nm]: E.dma_start(out=a[:, :], in_=b[:, :]),
                                       dma=True, dsem="c_" + nm)
        emit_stream_in(env, pb0, st, dram)
        pb0.build()

        phase_l1ffn(env, st, dram, tag)

        pbE = PB(env, "out")
        emit_stream_out(env, pbE, st, dram)
        pbE.build()
    nc.finalize()
    return nc


def phase_l1attn(env, st, dram):
    nc = env.nc
    with ExitStack() as es:
        st["m_sb"] = es.enter_context(nc.sbuf_tensor("am_sb", [1, TPC], F32))
        st["rstd_sb"] = es.enter_context(nc.sbuf_tensor("arstd_sb", [1, TPC], F32))
        run_ln(env, st, rms=False, tag="lnb")
    xln, stream = st["xln"], st["stream"]
    cosb, sinb = st["c_rope_cos"], st["c_rope_sin"]
    perm, m_up, m_lo = st["c_perm_qk"], st["c_mask_up"], st["c_mask_lo"]

    with ExitStack() as es:
        afm = es.enter_context(nc.sbuf_tensor("a_afm", [128, 4 * TPC], BF16))
        for half in range(2):
            with ExitStack() as eh:
                wsl = eh.enter_context(nc.sbuf_tensor("h_wsl%d" % half, [128, 8 * 1536], BF16))
                vtm = eh.enter_context(nc.sbuf_tensor("h_vtm%d" % half, [128, 16 * 512], BF16))
                qk = eh.enter_context(nc.sbuf_tensor("h_qk%d" % half, [128, TPC], BF16))
                sw = eh.enter_context(nc.sbuf_tensor("h_sw%d" % half, [128, TPC], BF16))
                rot = eh.enter_context(nc.sbuf_tensor("h_rot%d" % half, [128, TPC], BF16))
                kr = eh.enter_context(nc.sbuf_tensor("h_kr%d" % half, [64, TPC], BF16))
                f1 = eh.enter_context(nc.sbuf_tensor("h_f1%d" % half, [128, TTW], F32))
                f2 = eh.enter_context(nc.sbuf_tensor("h_f2%d" % half, [128, TTW], F32))
                smx = eh.enter_context(nc.sbuf_tensor("h_smx%d" % half, [128, 256], F32))
                pn = eh.enter_context(nc.sbuf_tensor("h_pn%d" % half, [128, 256], BF16))
                ptr = eh.enter_context(nc.sbuf_tensor("h_ptr%d" % half, [128, 256], BF16))
                r1 = eh.enter_context(nc.sbuf_tensor("h_r1%d" % half, [128, 1], F32))
                r3 = eh.enter_context(nc.sbuf_tensor("h_r3%d" % half, [128, 1], F32))
                ps_a = [eh.enter_context(nc.psum_tensor("h_psa%d_%d" % (half, i), [128, TTW], F32))
                        for i in range(2)]
                ps_up = eh.enter_context(nc.psum_tensor("h_psu%d" % half, [128, 128], F32))
                ps_lo = eh.enter_context(nc.psum_tensor("h_psl%d" % half, [128, 256], F32))
                ps_tr = eh.enter_context(nc.psum_tensor("h_pst%d" % half, [128, 256], BF16))
                ps_o = eh.enter_context(nc.psum_tensor("h_pso%d" % half, [128, 128], F32))
                pb = PB(env, "l1a%d" % half)

                wtoks = []
                slot_last = {}
                for kc in range(KC):
                    for gi, base in enumerate((half * 512, 1024 + half * 512,
                                               2048 + half * 512)):
                        slot = (kc * 3 + gi) % 4
                        t = pb.emit("gpsimd",
                                    lambda E, kc=kc, gi=gi, base=base: E.dma_start(
                            out=wsl[:, kc * 1536 + gi * 512: kc * 1536 + (gi + 1) * 512],
                            in_=dram["l1_wqkvT"][kc * 128:(kc + 1) * 128, base:base + 512]),
                            deps=[slot_last.get(slot)],
                            dma=True, dsem="qkv%d_%d" % (half, slot))
                        slot_last[slot] = t
                        wtoks.append(t)
                ev_prev = [None, None]
                v_ev = None
                for tc in range(16):
                    bank = tc % 2
                    tok = None
                    for kc in range(KC):
                        tok = pb.emit("tensor", lambda E, kc=kc, tc=tc, bank=bank,
                                      s=(kc == 0), p=(kc == KC - 1): E.matmul(
                            ps_a[bank][:, :],
                            xln[:, kc * TPC + tc * 128: kc * TPC + (tc + 1) * 128],
                            wsl[:, kc * 1536 + 1024: kc * 1536 + 1536],
                            start=s, stop=p),
                            deps=(wtoks + [st["tok_xln"][(tc * 128) // TTW],
                                           ev_prev[bank]] if kc == 0 else ()),
                            token=(kc == KC - 1))
                    v_ev = pb.emit("vector", lambda E, tc=tc, bank=bank: E.tensor_copy(
                        out=vtm[:, tc * 512:(tc + 1) * 512], in_=ps_a[bank][:, :]),
                        deps=[tok], token=True)
                    ev_prev[bank] = v_ev

                last_s_mm = st.get("tok_prev_wout")
                last_pm = st.get("tok_prev_wout")
                prev_hd = {"up": None, "lo": None, "tr": None, "o": st.get("tok_prev_wout")}
                for hl in range(8):
                    qk_ev = None
                    for tt in range(NTT):
                        bank = tt % 2
                        tok = None
                        for kc in range(KC):
                            dq = (wtoks + [st["tok_xln"][tt], ev_prev[bank], qk_ev]
                                  if kc == 0 else ())
                            pb.emit("tensor", lambda E, kc=kc, tt=tt, hl=hl, bank=bank,
                                    s=(kc == 0), p=(kc == KC - 1): E.matmul(
                                ps_a[bank][0:64, :],
                                wsl[:, kc * 1536 + hl * 64: kc * 1536 + (hl + 1) * 64],
                                xln[:, kc * TPC + tt * TTW: kc * TPC + (tt + 1) * TTW],
                                start=s, stop=p), deps=dq)
                        for kc in range(KC):
                            tok = pb.emit("tensor", lambda E, kc=kc, tt=tt, hl=hl, bank=bank,
                                          s=(kc == 0), p=(kc == KC - 1): E.matmul(
                                ps_a[bank][64:128, :],
                                wsl[:, kc * 1536 + 512 + hl * 64: kc * 1536 + 512 + (hl + 1) * 64],
                                xln[:, kc * TPC + tt * TTW: kc * TPC + (tt + 1) * TTW],
                                start=s, stop=p), deps=(), token=(kc == KC - 1))
                        qk_ev = pb.emit("vector", lambda E, tt=tt, bank=bank: E.tensor_copy(
                            out=qk[:, tt * TTW:(tt + 1) * TTW], in_=ps_a[bank][:, :]),
                            deps=[tok, last_pm], token=True)
                        ev_prev[bank] = qk_ev
                    sw_ev = None
                    for tt in range(NTT):
                        bank = tt % 2
                        t_pm = pb.emit("tensor", lambda E, tt=tt, bank=bank: E.matmul(
                            ps_a[bank][:, :], perm[:, :],
                            qk[:, tt * TTW:(tt + 1) * TTW], start=True, stop=True),
                            deps=[qk_ev, ev_prev[bank], st.get("tok_perm_qk")],
                            token=True)
                        last_pm = t_pm
                        sw_ev = pb.emit("vector", lambda E, tt=tt, bank=bank: E.tensor_copy(
                            out=sw[:, tt * TTW:(tt + 1) * TTW], in_=ps_a[bank][:, :]),
                            deps=[t_pm], token=True)
                        ev_prev[bank] = sw_ev
                    t_rot = None
                    for tt in range(NTT):
                        sl = slice(tt * TTW, (tt + 1) * TTW)
                        pb.emit("vector", lambda E, a=sl: E.tensor_tensor(
                            out=f1[:, :], in0=qk[:, a], in1=cosb[:, a], op=ALU.mult),
                            deps=[sw_ev, st.get("tok_rope_cos")])
                        pb.emit("vector", lambda E, a=sl: E.tensor_tensor(
                            out=f2[:, :], in0=sw[:, a], in1=sinb[:, a], op=ALU.mult),
                            deps=[st.get("tok_rope_sin")])
                        t_rot = pb.emit("vector", lambda E, a=sl: E.tensor_tensor(
                            out=rot[:, a], in0=f1[:, :], in1=f2[:, :], op=ALU.add),
                            deps=[last_s_mm], token=True)
                    t_kr = pb.emit("vector", lambda E: E.tensor_copy(
                        out=kr[:, :], in_=rot[64:128, :]), deps=[t_rot, last_s_mm],
                        token=True)

                    for bb in range(8):
                        q0 = bb * 256
                        t_su = pb.emit("tensor", lambda E, q0=q0: E.matmul(
                            ps_up[:, :], rot[0:64, q0:q0 + 128], kr[:, q0:q0 + 128],
                            start=True, stop=True),
                            deps=[t_kr, prev_hd["up"]], token=True)
                        t_sm = pb.emit("vector", lambda E: E.scalar_tensor_tensor(
                            out=smx[:, 0:128], in0=ps_up[:, :], scalar=0.125,
                            in1=m_up[:, :], op0=ALU.mult, op1=ALU.add),
                            deps=[t_su, st.get("tok_mask_up")], token=True)
                        prev_hd["up"] = t_sm
                        t_mx = pb.emit("vector", lambda E: E.reduce_max(
                            r1[:, :], smx[:, 0:128], axis=AX.X, negate=True))
                        t_p = pb.emit("scalar", lambda E: E.activation(
                            out=smx[:, 0:128], in_=smx[:, 0:128], func=AF.Exp,
                            bias=r1[:, 0:1]), deps=[t_mx], token=True)
                        t_sum = pb.emit("vector", lambda E: E.reduce_sum(
                            r1[:, :], smx[:, 0:128], axis=AX.X), deps=[t_p])
                        t_rc = pb.emit("vector", lambda E: E.reciprocal(
                            r3[:, :], r1[:, :]))
                        t_pn = pb.emit("vector", lambda E: E.tensor_scalar(
                            out=pn[:, 0:128], in0=smx[:, 0:128], scalar1=r3[:, 0:1],
                            scalar2=None, op0=ALU.mult), token=True)
                        t_tr = pb.emit("tensor", lambda E: E.transpose(
                            ps_tr[:, 0:128], pn[:, 0:128], st["c_ident_bf"][:, :]),
                            deps=[t_pn, prev_hd["tr"], st.get("tok_ident_bf")], token=True)
                        t_ptr = pb.emit("vector", lambda E: E.tensor_copy(
                            out=ptr[:, 0:128], in_=ps_tr[:, 0:128]),
                            deps=[t_tr, prev_hd["o"]], token=True)
                        prev_hd["tr"] = t_ptr
                        t_o = pb.emit("tensor", lambda E, q0=q0, hl=hl: E.matmul(
                            ps_o[0:64, :], vtm[:, (2 * (q0 // 256)) * 512 + hl * 64:
                                                (2 * (q0 // 256)) * 512 + (hl + 1) * 64],
                            ptr[:, 0:128], start=True, stop=True),
                            deps=[t_ptr, v_ev, prev_hd["o"]], token=True)
                        t_oe = pb.emit("vector", lambda E, q0=q0, hl=hl: E.tensor_copy(
                            out=afm[(hl % 2) * 64:(hl % 2) * 64 + 64,
                                    (hl // 2) * TPC + q0: (hl // 2) * TPC + q0 + 128],
                            in_=ps_o[0:64, :]), deps=[t_o], token=True)
                        prev_hd["o"] = t_oe
                        t_sl = pb.emit("tensor", lambda E, q0=q0: E.matmul(
                            ps_lo[:, :], rot[0:64, q0 + 128:q0 + 256], kr[:, q0:q0 + 256],
                            start=True, stop=True),
                            deps=[t_kr, prev_hd["lo"]], token=True)
                        t_sm2 = pb.emit("vector", lambda E: E.scalar_tensor_tensor(
                            out=smx[:, 0:256], in0=ps_lo[:, :], scalar=0.125,
                            in1=m_lo[:, :], op0=ALU.mult, op1=ALU.add),
                            deps=[t_sl, st.get("tok_mask_lo")], token=True)
                        prev_hd["lo"] = t_sm2
                        t_mx = pb.emit("vector", lambda E: E.reduce_max(
                            r1[:, :], smx[:, 0:256], axis=AX.X, negate=True))
                        t_p = pb.emit("scalar", lambda E: E.activation(
                            out=smx[:, 0:256], in_=smx[:, 0:256], func=AF.Exp,
                            bias=r1[:, 0:1]), deps=[t_mx], token=True)
                        t_sum = pb.emit("vector", lambda E: E.reduce_sum(
                            r1[:, :], smx[:, 0:256], axis=AX.X), deps=[t_p])
                        t_rc = pb.emit("vector", lambda E: E.reciprocal(
                            r3[:, :], r1[:, :]))
                        t_pn = pb.emit("vector", lambda E: E.tensor_scalar(
                            out=pn[:, 0:256], in0=smx[:, 0:256], scalar1=r3[:, 0:1],
                            scalar2=None, op0=ALU.mult), token=True)
                        t_tra = pb.emit("tensor", lambda E: E.transpose(
                            ps_tr[:, 0:128], pn[:, 0:128], st["c_ident_bf"][:, :]),
                            deps=[t_pn, prev_hd["tr"], prev_hd["o"]], token=True)
                        t_trb = pb.emit("tensor", lambda E: E.transpose(
                            ps_tr[:, 128:256], pn[:, 128:256], st["c_ident_bf"][:, :]),
                            token=True)
                        t_ptr = pb.emit("vector", lambda E: E.tensor_copy(
                            out=ptr[:, :], in_=ps_tr[:, :]),
                            deps=[t_tra, t_trb, prev_hd["o"]], token=True)
                        prev_hd["tr"] = t_ptr
                        tok_o = None
                        for j in range(2):
                            tok_o = pb.emit("tensor", lambda E, q0=q0, hl=hl, j=j,
                                            s=(j == 0), p=(j == 1): E.matmul(
                                ps_o[64:128, :],
                                vtm[:, (2 * (q0 // 256) + j) * 512 + hl * 64:
                                      (2 * (q0 // 256) + j) * 512 + (hl + 1) * 64],
                                ptr[:, j * 128:(j + 1) * 128], start=s, stop=p),
                                deps=([t_ptr, v_ev, prev_hd["o"]] if j == 0 else ()),
                                token=(j == 1))
                        last_s_mm = tok_o
                        t_oe = pb.emit("vector", lambda E, q0=q0, hl=hl: E.tensor_copy(
                            out=afm[(hl % 2) * 64:(hl % 2) * 64 + 64,
                                    (hl // 2) * TPC + q0 + 128: (hl // 2) * TPC + q0 + 256],
                            in_=ps_o[64:128, :]), deps=[tok_o], token=True)
                        prev_hd["o"] = t_oe
                st["tok_afm%d" % half] = prev_hd["o"]
                pb.build()

            with ExitStack() as eh:
                wo = eh.enter_context(nc.sbuf_tensor("w_wo%d" % half, [128, 8 * 1024], BF16))
                ps_w = [eh.enter_context(nc.psum_tensor("w_psw%d_%d" % (half, i),
                                                        [128, TTW], F32)) for i in range(2)]
                pb = PB(env, "l1w%d" % half)
                wtoks = []
                slot_last = {}
                for kc in range(KC):
                    slot = kc % 4
                    t = pb.emit("gpsimd", lambda E, kc=kc: E.dma_start(
                        out=wo[:, kc * 1024:(kc + 1) * 1024],
                        in_=dram["l1_woutT"][kc * 128:(kc + 1) * 128, :]),
                        deps=[slot_last.get(slot)],
                        dma=True, dsem="wo%d_%d" % (half, slot))
                    slot_last[slot] = t
                    wtoks.append(t)
                prev_ev = [None, None]
                for dc in range(8):
                    for tt in range(NTT):
                        bank = (dc * NTT + tt) % 2
                        tok = None
                        for c in range(4):
                            kc = half * 4 + c
                            tok = pb.emit("tensor", lambda E, c=c, kc=kc, dc=dc, tt=tt,
                                          bank=bank, s=(c == 0), p=(c == 3): E.matmul(
                                ps_w[bank][:, :],
                                wo[:, kc * 1024 + dc * 128: kc * 1024 + (dc + 1) * 128],
                                afm[:, c * TPC + tt * TTW: c * TPC + (tt + 1) * TTW],
                                start=s, stop=p),
                                deps=(wtoks + [st["tok_afm%d" % half], prev_ev[bank]]
                                      if c == 0 else ()),
                                token=(c == 3))
                        prev_ev[bank] = pb.emit("vector",
                                                lambda E, dc=dc, tt=tt, bank=bank: E.tensor_tensor(
                            out=stream[:, dc * TPC + tt * TTW: dc * TPC + (tt + 1) * TTW],
                            in0=stream[:, dc * TPC + tt * TTW: dc * TPC + (tt + 1) * TTW],
                            in1=ps_w[bank][:, :], op=ALU.add), deps=[tok], token=True)
                        st["tok_prev_wout"] = tok
                pb.build()


L1ATTN_WNAMES = ["l1_wqkvT", "l1_woutT"]
L1ATTN_CNAMES = ["ones128x1_f32", "ones128x1_bf", "ones1x128_f32",
                 "ident_bf", "mask_up", "mask_lo", "rope_cos", "rope_sin",
                 "perm_qk"]


def build_l1attn_program():
    nc = bass.Bass("TRN2", target_bir_lowering=False)
    es = ExitStack()
    env = Env(nc, es)
    with es:
        dram = declare_io(nc, L1ATTN_WNAMES, L1ATTN_CNAMES)
        st = {}
        st["stream"] = env.sbuf("stream", [128, KC * TPC], F32)
        st["xln"] = env.sbuf("xln", [128, KC * TPC], BF16)

        pb0 = PB(env, "boot")
        reg_scalar_consts(env, pb0)
        load_consts(env, pb0, st, dram, L1ATTN_CNAMES)
        emit_stream_in(env, pb0, st, dram)
        pb0.build()

        phase_l1attn(env, st, dram)

        pbE = PB(env, "out")
        emit_stream_out(env, pbE, st, dram)
        pbE.build()
    nc.finalize()
    return nc
